# version 07
# baseline (speedup 1.0000x reference)
import os

os.environ.setdefault("NEURON_SCRATCHPAD_PAGE_SIZE", "1024")

import numpy as np
import concourse.bass as bass
import concourse.mybir as mybir
from concourse.bass_utils import run_bass_kernel_spmd

F32 = mybir.dt.float32
BF16 = mybir.dt.bfloat16
AF = mybir.ActivationFunctionType
ALU = mybir.AluOpType
AX = mybir.AxisListType

H = 2048
HC = H // 128
I = 768
IC = I // 128
E = 32
T = 1024
GRP = 4
INV_SQRT_H = float(1.0 / np.sqrt(np.float32(H)))
LN_EPS = 1e-5

DEBUG_STAGE = 3
ATTN_UNTIL = 4
Q_STREAMS = ()


class Op:
    __slots__ = ("eng", "fn", "deps", "inc", "amt", "sem", "val", "waits",
                 "cc")

    def __init__(self, eng, fn, amt):
        self.eng = eng
        self.fn = fn
        self.amt = amt
        self.deps = ()
        self.inc = False
        self.sem = None
        self.val = 0
        self.waits = ()
        self.cc = False


class Buf:
    __slots__ = ("name", "w", "r")

    def __init__(self, name):
        self.name = name
        self.w = None
        self.r = {}


class Bld:
    ENGS = ("pe", "act", "dve", "pool", "sp")

    def __init__(self, nc):
        self.nc = nc
        self.ops = []
        self.pending = {}
        self.bufs = []
        self.last_eng = {}
        self.qi = 0

    def q(self, tag=None):
        if tag not in Q_STREAMS:
            return "sp"
        self.qi += 1
        return "sp" if self.qi % 2 == 0 else "act"

    def buf(self, name):
        b = Buf(name)
        self.bufs.append(b)
        return b

    def emit(self, eng, fn, reads=(), writes=(), deps=(), dma=False, cc=False):
        op = Op(eng, fn, 16 if dma else 1)
        op.cc = cc
        dd = {}

        def add(d):
            if d is None or d is op:
                return
            if d.eng == "pe" and eng == "pe":
                return
            dd[id(d)] = d

        for b in reads:
            add(b.w)
        for b in writes:
            add(b.w)
            for d in b.r.values():
                add(d)
        for d in deps:
            add(d)
        for d in self.pending.pop(eng, ()):
            add(d)
        op.deps = list(dd.values())
        for b in reads:
            b.r[eng] = op
        for b in writes:
            b.w = op
            b.r = {}
        self.ops.append(op)
        self.last_eng[eng] = op
        return op

    def fence_deps(self):
        return [o for o in self.last_eng.values()]

    def barrier(self):
        for e in self.ENGS:
            prev = list(self.pending.get(e, ()))
            self.pending[e] = prev + [op for k, op in self.last_eng.items()
                                      if k != e]
        for b in self.bufs:
            b.w = None
            b.r = {}

    def finalize(self):
        for op in self.ops:
            for d in op.deps:
                d.inc = True
        sems = {}
        tot = {}
        cnt = {}
        NS = 8
        dma_pools = {}
        for op in self.ops:
            if op.amt == 16:
                op.inc = True
            if not op.inc:
                continue
            if op.cc:
                e = "cc"
            elif op.amt == 16:
                key = op.eng + "_d"
                if key not in dma_pools:
                    dma_pools[key] = [
                        [self.nc.ctx.enter_context(
                            self.nc.semaphore(f"s_{key}_{i}"))
                         for i in range(NS)], 0]
                pool, c = dma_pools[key]
                op.sem = pool[c % NS]
                op.val = 16 * (c // NS + 1)
                dma_pools[key][1] = c + 1
                continue
            else:
                e = op.eng + "_c"
            if e not in sems or tot[e] + op.amt > 48000:
                cnt[e] = cnt.get(e, 0) + 1
                sems[e] = self.nc.ctx.enter_context(
                    self.nc.semaphore(f"s_{e}_{cnt[e]}"))
                tot[e] = 0
            tot[e] += op.amt
            op.sem = sems[e]
            op.val = tot[e]
        waited = {e: {} for e in self.ENGS}
        for op in self.ops:
            w = {}
            for d in op.deps:
                key = id(d.sem)
                if w.get(key, (None, 0))[1] < d.val:
                    w[key] = (d.sem, d.val)
            ws = []
            for key, (sem, val) in w.items():
                if waited[op.eng].get(key, 0) >= val:
                    continue
                waited[op.eng][key] = val
                ws.append((sem, val))
            op.waits = ws

    def run_block(self):
        self.finalize()
        per = {e: [o for o in self.ops if o.eng == e] for e in self.ENGS}
        nc = self.nc
        with nc.Block() as block:
            def mk(ops):
                def body(eng):
                    for op in ops:
                        for sem, val in op.waits:
                            eng.wait_ge(sem, val)
                        ins = op.fn(eng)
                        if op.inc:
                            ins.then_inc(op.sem, op.amt)
                return body
            block.tensor(mk(per["pe"]))
            block.scalar(mk(per["act"]))
            block.vector(mk(per["dve"]))
            block.gpsimd(mk(per["pool"]))
            block.sync(mk(per["sp"]))


class Arena:
    def __init__(self, nc, name, total_bytes, dtype):
        esz = 2 if dtype == BF16 else 4
        self.n = total_bytes // esz
        self.t = nc.alloc_sbuf_tensor(name, [128, self.n], dtype).ap()
        self.off = 0
        self.marks = []

    def alloc(self, width):
        a = self.t[:, self.off:self.off + width]
        self.off += width
        assert self.off <= self.n, f"arena overflow {self.off} > {self.n}"
        return a

    def mark(self):
        self.marks.append(self.off)

    def reset(self):
        self.off = self.marks.pop()


def mm_fn(o, l, r, st, sp):
    return lambda t: t.matmul(o, l, r, start=st, stop=sp)


def dma_fn(o, i, tr=False):
    if tr:
        return lambda s: s.dma_start(out=o, in_=i, transpose=True)
    return lambda s: s.dma_start(out=o, in_=i)


def tr_fn(o, i, ident):
    return lambda t: t.transpose(o, i, ident)


def copy_fn(o, i):
    return lambda v: v.tensor_copy(o, i)


def build_nc(stage=3):
    nc = bass.Bass("TRN2", target_bir_lowering=False)

    x_in = nc.dram_tensor("x", [H, T], F32, kind="ExternalInput").ap()
    wqkv = nc.dram_tensor("wqkv", [3 * H, H], F32, kind="ExternalInput").ap()
    bqkv = nc.dram_tensor("bqkv", [3 * H], F32, kind="ExternalInput").ap()
    wout = nc.dram_tensor("wout", [H, H], F32, kind="ExternalInput").ap()
    bout = nc.dram_tensor("bout", [H], F32, kind="ExternalInput").ap()
    gw = nc.dram_tensor("gw", [E, H], F32, kind="ExternalInput").ap()
    wg = nc.dram_tensor("wg", [E, I, H], F32, kind="ExternalInput").ap()
    wu = nc.dram_tensor("wu", [E, I, H], F32, kind="ExternalInput").ap()
    wd = nc.dram_tensor("wd", [E, H, I], F32, kind="ExternalInput").ap()
    out = nc.dram_tensor("out", [H, T], F32, kind="ExternalOutput").ap()

    kv_in = nc.dram_tensor("kv_in", [2 * H, T], BF16, kind="Internal").ap()
    kv_out = nc.dram_tensor("kv_out", [GRP * 2 * H, T], BF16,
                            kind="Internal").ap()
    wqkv_b = nc.dram_tensor("wqkv_b", [3 * H, H], BF16, kind="Internal").ap()
    wout_b = nc.dram_tensor("wout_b", [H, H], BF16, kind="Internal").ap()
    gw_b = nc.dram_tensor("gw_b", [E, H], BF16, kind="Internal").ap()
    wg_b = nc.dram_tensor("wg_b", [E, I, H], BF16, kind="Internal").ap()
    wu_b = nc.dram_tensor("wu_b", [E, I, H], BF16, kind="Internal").ap()
    wd_b = nc.dram_tensor("wd_b", [E, H, I], BF16, kind="Internal").ap()
    wguT_c = nc.dram_tensor("wguT_c", [2, E, 2, 128, HC * 384], BF16,
                            kind="Internal").ap()
    wdT_c = nc.dram_tensor("wdT_c", [E, 4, 128, IC * 512], BF16,
                           kind="Internal").ap()
    v_tm = nc.dram_tensor("v_tm", [GRP * T, H], BF16, kind="Internal").ap()
    mw_dram = nc.dram_tensor("mw_dram", [E, T], F32, kind="Internal").ap()
    ln_rows = nc.dram_tensor("ln_rows", [2, T], F32, kind="Internal").ap()

    b = Bld(nc)

    B_wgb = [b.buf(f"wgb{e}") for e in range(E)]
    B_wub = [b.buf(f"wub{e}") for e in range(E)]
    B_wdb = [b.buf(f"wdb{e}") for e in range(E)]
    B_wqkvb = [b.buf(f"wqkvb{i}") for i in range(12)]
    B_woutb = [b.buf(f"woutb{i}") for i in range(4)]
    B_gwb = b.buf("gwb")
    B_wguT = [[b.buf(f"wguT{m}_{e}") for e in range(E)] for m in range(2)]
    B_wdT = [b.buf(f"wdT{e}") for e in range(E)]
    B_vtm = b.buf("vtm")
    B_mwd = b.buf("mwd")
    B_lnr = b.buf("lnr")
    B_kvin = b.buf("kvin")
    B_kvout = b.buf("kvout")
    B_out = b.buf("outd")

    psA = nc.ctx.enter_context(nc.psum_tensor("psA", [128, 1024], F32)).ap()
    psB = nc.ctx.enter_context(nc.psum_tensor("psB", [128, 1024], F32)).ap()
    psC = nc.ctx.enter_context(nc.psum_tensor("psC", [128, 1024], F32)).ap()
    psD = nc.ctx.enter_context(nc.psum_tensor("psD", [128, 1024], F32)).ap()
    B_A = b.buf("psA")
    B_B = b.buf("psB")
    B_C = b.buf("psC")
    B_D = b.buf("psD")
    B_Ca = b.buf("psCa")
    B_Cb = b.buf("psCb")
    B_Da = b.buf("psDa")
    B_Db = b.buf("psDb")

    def sb(name, shape, dt):
        return nc.alloc_sbuf_tensor(name, shape, dt).ap()

    X = [sb(f"X{hc}", [128, T], F32) for hc in range(HC)]
    B_X = [b.buf(f"X{hc}") for hc in range(HC)]
    ident = sb("ident", [128, 128], F32)
    identb = sb("identb", [128, 128], BF16)
    ones_col = sb("ones_col", [128, 1], BF16)
    bq_cols = sb("bq_cols", [128, 48], F32)
    bo_cols = sb("bo_cols", [128, 16], F32)
    l_sb = sb("l_sb", [128, E], F32)
    mw_sb = sb("mw_sb", [128, E], F32)
    gwT = [sb(f"gwT{hc}", [128, E], BF16) for hc in range(HC)]
    B_gwT = [b.buf(f"gwT{hc}") for hc in range(HC)]
    topk8 = sb("topk8", [128, 8], F32)
    zap = sb("zap", [128, E], F32)
    selm = sb("selm", [128, E], F32)
    expl = sb("expl", [128, E], F32)
    nmax = sb("nmax", [128, 1], F32)
    zsum = sb("zsum", [128, 1], F32)
    rz1 = sb("rz1", [128, 1], F32)
    eps_c = sb("eps_c", [128, 1], F32)
    B_tk = b.buf("topk")
    B_ident = b.buf("ident")
    B_identb = b.buf("identb")
    B_ones = b.buf("ones")
    B_bq = b.buf("bq")
    B_bo = b.buf("bo")

    AB = Arena(nc, "arena_bf", 99 * 1024, BF16)
    AR = Arena(nc, "arena_f32", 40 * 1024, F32)

    for hc in range(HC):
        b.emit("pool", dma_fn(X[hc], x_in[hc * 128:(hc + 1) * 128, :]),
               writes=[B_X[hc]], dma=True)
    b.emit("pool", lambda g: g.memset(ident[:, :], 0.0), writes=[B_ident])
    b.emit("pool", lambda g: g.affine_select(
        out=ident[:, :], in_=ident[:, :], compare_op=ALU.not_equal,
        fill=1.0, base=0, pattern=[[-1, 128]], channel_multiplier=1),
        reads=[B_ident], writes=[B_ident])
    b.emit("dve", copy_fn(identb[:, :], ident[:, :]),
           reads=[B_ident], writes=[B_identb])
    b.emit("pool", lambda g: g.memset(ones_col[:, :], 1.0), writes=[B_ones])
    b.emit("pool", lambda g: g.memset(eps_c[:, :], LN_EPS), writes=[B_ones])
    b.emit("pool", dma_fn(gw_b[:, :], gw[:, :]), writes=[B_gwb], dma=True)

    cast_done = [False] * E

    def emit_casts(e):
        if cast_done[e]:
            return
        cast_done[e] = True
        b.emit("pool", dma_fn(wg_b[e, :, :], wg[e, :, :]),
               writes=[B_wgb[e]], dma=True)
        b.emit("pool", dma_fn(wd_b[e, :, :], wd[e, :, :]),
               writes=[B_wdb[e]], dma=True)

    emit_casts(0)
    emit_casts(1)

    for j in range(48):
        b.emit("pool", dma_fn(bq_cols[:, j:j + 1],
                              bqkv[j * 128:(j + 1) * 128, None]),
               writes=[B_bq], dma=True)
    for j in range(16):
        b.emit("pool", dma_fn(bo_cols[:, j:j + 1],
                              bout[j * 128:(j + 1) * 128, None]),
               writes=[B_bo], dma=True)

    qkv_cast_emitted = [False]

    def emit_qkv_casts():
        if qkv_cast_emitted[0]:
            return
        qkv_cast_emitted[0] = True
        for i in range(12):
            b.emit("pool", dma_fn(wqkv_b[i * 512:(i + 1) * 512, :],
                                  wqkv[i * 512:(i + 1) * 512, :]),
                   writes=[B_wqkvb[i]], dma=True)
        for i in range(4):
            b.emit("pool", dma_fn(wout_b[i * 512:(i + 1) * 512, :],
                                  wout[i * 512:(i + 1) * 512, :]),
                   writes=[B_woutb[i]], dma=True)

    for hc in range(HC):
        b.emit(b.q("gw"), dma_fn(gwT[hc][:, :], gw_b[:, hc * 128:(hc + 1) * 128],
                            tr=True),
               reads=[B_gwb], writes=[B_gwT[hc]], dma=True)

    def emit_ln(xn, B_xn, xh, B_xh, sq, B_sq, t1, B_t1, Ra, Rb, Rc, B_rows,
                rbb, B_rb, mrb, B_mrb, ps_x, B_px, ps_sq, B_psq, fence=()):
        first = dict(deps=list(fence))
        for hc in range(HC):
            b.emit("dve", copy_fn(xh, X[hc]), reads=[B_X[hc]], writes=[B_xh],
                   **(first if hc == 0 else {}))
            b.emit("act", (lambda o, i: lambda s: s.activation(o, i, AF.Square))(
                sq, xh), reads=[B_xh], writes=[B_sq])
            for tn in range(2):
                sl = slice(tn * 512, (tn + 1) * 512)
                b.emit("pe", mm_fn(ps_x[0:1, sl], ones_col[:, :], xh[:, sl],
                                   hc == 0, hc == HC - 1),
                       reads=[B_xh, B_ones], writes=[B_px])
                b.emit("pe", mm_fn(ps_sq[0:1, sl], ones_col[:, :], sq[:, sl],
                                   hc == 0, hc == HC - 1),
                       reads=[B_sq, B_ones], writes=[B_psq])
        b.emit("dve", lambda v: v.tensor_scalar(
            out=Ra, in0=ps_x[0:1, :], scalar1=1.0 / H, scalar2=None,
            op0=ALU.mult), reads=[B_px], writes=[B_rows], deps=list(fence))
        b.emit("dve", lambda v: v.tensor_scalar(
            out=Rb, in0=ps_sq[0:1, :], scalar1=1.0 / H, scalar2=None,
            op0=ALU.mult), reads=[B_psq], writes=[B_rows])
        b.emit("dve", lambda v: v.tensor_mul(Rc, Ra, Ra),
               reads=[B_rows], writes=[B_rows])
        b.emit("dve", lambda v: v.tensor_sub(Rb, Rb, Rc),
               reads=[B_rows], writes=[B_rows])
        b.emit("act", lambda s: s.activation(Rc, Rb, AF.Sqrt,
                                             bias=eps_c[0:1, :]),
               reads=[B_rows, B_ones], writes=[B_rows])
        b.emit("dve", lambda v: v.reciprocal(Rb, Rc),
               reads=[B_rows], writes=[B_rows])
        b.emit("dve", lambda v: v.tensor_mul(Rc, Ra, Rb),
               reads=[B_rows], writes=[B_rows])
        b.emit("pool", dma_fn(ln_rows[0:1, :], Rb),
               reads=[B_rows], writes=[B_lnr], dma=True)
        b.emit("pool", dma_fn(ln_rows[1:2, :], Rc),
               reads=[B_rows], writes=[B_lnr], dma=True)
        b.emit("pool", dma_fn(rbb, ln_rows[0:1, :].partition_broadcast(128)),
               reads=[B_lnr], writes=[B_rb], dma=True, deps=list(fence))
        b.emit("pool", dma_fn(mrb, ln_rows[1:2, :].partition_broadcast(128)),
               reads=[B_lnr], writes=[B_mrb], dma=True)
        for hc in range(HC):
            b.emit("dve", (lambda o, i: lambda v: v.tensor_mul(o, i, rbb))(
                t1, X[hc]), reads=[B_X[hc], B_rb], writes=[B_t1],
                **(first if hc == 0 else {}))
            b.emit("dve", (lambda o, i: lambda v: v.tensor_sub(o, i, mrb))(
                xn[hc], t1), reads=[B_t1, B_mrb], writes=[B_xn[hc]])

    def emit_router(xn, B_xn, ps_l, B_pl, ps_t, B_pt, ltT, B_ltT, mwT, B_mwT,
                    fence=()):
        for hc in range(HC):
            for tn in range(2):
                sl = slice(tn * 512, (tn + 1) * 512)
                b.emit("pe", mm_fn(ps_l[0:E, sl], gwT[hc][:, :], xn[hc][:, sl],
                                   hc == 0, hc == HC - 1),
                       reads=[B_gwT[hc], B_xn[hc]], writes=[B_pl])
        b.emit("dve", copy_fn(ltT, ps_l[0:E, :]), reads=[B_pl],
               writes=[B_ltT], deps=list(fence))
        for tt in range(8):
            b.emit("pe", tr_fn(ps_t[:, 0:E], ltT[:, tt * 128:(tt + 1) * 128],
                               ident[0:E, 0:E]),
                   reads=[B_ltT, B_ident], writes=[B_pt])
            b.emit("dve", copy_fn(l_sb[:, :], ps_t[:, 0:E]),
                   reads=[B_pt], writes=[B_tk])
            b.emit("dve", lambda v: v.max(out=topk8[:, :], in_=l_sb[:, :]),
                   reads=[B_tk], writes=[B_tk])
            b.emit("dve", lambda v: v.match_replace(
                out=zap[:, :], in_to_replace=topk8[:, :], in_values=l_sb[:, :],
                imm_value=-1e30), reads=[B_tk], writes=[B_tk])
            b.emit("dve", lambda v: v.tensor_sub(selm[:, :], l_sb[:, :],
                                                 zap[:, :]),
                   reads=[B_tk], writes=[B_tk])
            b.emit("dve", lambda v: v.tensor_scalar_min(selm[:, :], selm[:, :],
                                                        1.0),
                   reads=[B_tk], writes=[B_tk])
            b.emit("dve", lambda v: v.reduce_max(nmax[:, :], l_sb[:, :],
                                                 axis=AX.X, negate=True),
                   reads=[B_tk], writes=[B_tk])
            b.emit("act", lambda s: s.activation(expl[:, :], l_sb[:, :], AF.Exp,
                                                 bias=nmax[:, :]),
                   reads=[B_tk], writes=[B_tk])
            b.emit("dve", lambda v: v.tensor_mul(expl[:, :], expl[:, :],
                                                 selm[:, :]),
                   reads=[B_tk], writes=[B_tk])
            b.emit("dve", lambda v: v.reduce_sum(zsum[:, :], expl[:, :],
                                                 axis=AX.X),
                   reads=[B_tk], writes=[B_tk])
            b.emit("dve", lambda v: v.reciprocal(rz1[:, :], zsum[:, :]),
                   reads=[B_tk], writes=[B_tk])
            b.emit("dve", lambda v: v.tensor_scalar(
                out=mw_sb[:, :], in0=expl[:, :], scalar1=rz1[:, :],
                scalar2=None, op0=ALU.mult), reads=[B_tk], writes=[B_tk])
            b.emit("pe", tr_fn(ps_t[0:E, 0:128], mw_sb[:, :], ident[:, :]),
                   reads=[B_tk, B_ident], writes=[B_pt])
            b.emit("dve", (lambda c: lambda v: v.tensor_copy(
                mwT[:, c * 128:(c + 1) * 128], ps_t[0:E, 0:128]))(tt),
                reads=[B_pt], writes=[B_mwT])
        b.emit("pool", dma_fn(mw_dram[:, :], mwT), reads=[B_mwT],
               writes=[B_mwd], dma=True)

    def emit_ffn(xn, B_xn, gu_half, B_guh, dn_half, B_dnh, actw, B_actw,
                 mwb, B_mwb, g_s, B_gs, prod, B_prod, uf32, B_uf32,
                 first_call, fence):
        gu_tiles = {}
        dn_tiles = {}
        GU_SUB = [(0, 0), (1, 0), (0, 1), (1, 1)]

        def gu_gen(g):
            if g >= 4 * E:
                return
            e, sub = divmod(g, 4)
            mat, ih = GU_SUB[sub]
            half = g % 2
            slab, bufs = gu_half[half], B_guh[half]
            for hc in range(HC):
                gu_tiles[(e, mat, ih, hc)] = (
                    slab[:, hc * 384:(hc + 1) * 384], bufs[hc])
            cdst = wguT_c[mat, e, ih]
            if first_call and mat == 0:
                for hc in range(HC):
                    srca = wg_b[e, ih * 384:(ih + 1) * 384,
                                hc * 128:(hc + 1) * 128]
                    b.emit("sp", dma_fn(slab[:, hc * 384:(hc + 1) * 384],
                                        srca, tr=True),
                           reads=[B_wgb[e]], writes=[bufs[hc]], dma=True)
                b.emit("pool", dma_fn(cdst, slab),
                       reads=list(bufs), writes=[B_wguT[mat][e]], dma=True)
            elif first_call:
                for j3 in range(3):
                    ft, fb = uf32[j3 % 2], B_uf32[j3 % 2]
                    b.emit("pool", dma_fn(
                        ft, wu[e, ih * 384 + j3 * 128:
                               ih * 384 + (j3 + 1) * 128, :]),
                        writes=[fb], dma=True)
                    for hc in range(HC):
                        j = hc * 3 + j3
                        tp, B_tp = ((psC[:, 0:128], B_C) if j % 2 == 0
                                    else (psD[:, 0:128], B_D))
                        b.emit("pe", (lambda o, i: lambda t: t.transpose(
                            o, i, ident[:, :]))(
                            tp, ft[:, hc * 128:(hc + 1) * 128]),
                            reads=[fb, B_ident], writes=[B_tp])
                        b.emit("dve", (lambda o, i: lambda v: v.tensor_copy(
                            o, i))(
                            slab[:, hc * 384 + j3 * 128:
                                 hc * 384 + (j3 + 1) * 128], tp),
                            reads=[B_tp],
                            writes=[bufs[hc]] if j3 == 2 else [bufs[hc]])
                b.emit("pool", dma_fn(cdst, slab),
                       reads=list(bufs), writes=[B_wguT[mat][e]], dma=True)
            else:
                b.emit("sp", dma_fn(slab, cdst),
                       reads=[B_wguT[mat][e]], writes=list(bufs), dma=True)

        def dn_gen(g):
            if g >= 4 * E:
                return
            e, hcg2 = divmod(g, 4)
            half = g % 2
            slab, bufs = dn_half[half], B_dnh[half]
            for ic in range(IC):
                for hcl2 in range(2):
                    dn_tiles[(e, hcg2 * 2 + hcl2, ic)] = (
                        slab[:, ic * 512 + hcl2 * 256:
                             ic * 512 + (hcl2 + 1) * 256], bufs[ic])
            cdst = wdT_c[e, hcg2]
            if first_call:
                for ic in range(IC):
                    srca = wd_b[e, hcg2 * 512:(hcg2 + 1) * 512,
                                ic * 128:(ic + 1) * 128]
                    b.emit("sp", dma_fn(slab[:, ic * 512:(ic + 1) * 512],
                                        srca, tr=True),
                           reads=[B_wdb[e]], writes=[bufs[ic]], dma=True)
                b.emit("pool", dma_fn(cdst, slab),
                       reads=list(bufs), writes=[B_wdT[e]], dma=True)
            else:
                b.emit("sp", dma_fn(slab, cdst),
                       reads=[B_wdT[e]], writes=list(bufs), dma=True)

        def mw_stream(e):
            b.emit("pool", (lambda ee: lambda g: g.dma_start(
                out=mwb[ee % 2],
                in_=mw_dram[ee:ee + 1, :].partition_broadcast(128)))(e),
                reads=[B_mwd], writes=[B_mwb[e % 2]], dma=True,
                deps=(fence if e < 2 else ()))

        gu_gen(0)
        gu_gen(1)
        dn_gen(0)
        dn_gen(1)
        mw_stream(0)
        mw_stream(1)

        def gu_block(e):
            par = e % 2
            for ic in range(IC):
                ih, icl = ic // 3, ic % 3
                for mat, ps, B_ps in ((0, psA, B_A), (1, psB, B_B)):
                    for hc in range(HC):
                        ap, bb = gu_tiles[(e, mat, ih, hc)]
                        lhs = ap[:, icl * 128:(icl + 1) * 128]
                        for tn in range(2):
                            sl = slice(tn * 512, (tn + 1) * 512)
                            b.emit("pe", mm_fn(ps[:, sl], lhs, xn[hc][:, sl],
                                               hc == 0, hc == HC - 1),
                                   reads=[bb, B_xn[hc]], writes=[B_ps])
                b.emit("act", (lambda gg: lambda s: s.activation(
                    gg, psA[:, :], AF.Silu))(g_s),
                    reads=[B_A], writes=[B_gs],
                    deps=(fence if e == 0 and ic == 0 else ()))
                b.emit("dve", (lambda gg, pp: lambda v: v.scalar_tensor_tensor(
                    out=pp, in0=psB[:, :], scalar=1.0, in1=gg,
                    op0=ALU.mult, op1=ALU.mult))(g_s, prod),
                    reads=[B_B, B_gs], writes=[B_prod],
                    deps=(fence if e == 0 and ic == 0 else ()))
                b.emit("dve", (lambda o, pp, mb: lambda v: v.tensor_mul(
                    o, pp, mb))(actw[par][ic], prod, mwb[par]),
                    reads=[B_prod, B_mwb[par]], writes=[B_actw[par][ic]])
                if ic == 2:
                    gu_gen(4 * e + 2)
                    gu_gen(4 * e + 3)
                if ic == 5:
                    gu_gen(4 * e + 4)
                    gu_gen(4 * e + 5)

        def dn_block(e):
            par = e % 2
            for hcg in range(8):
                for hcl in range(2):
                    hc = hcg * 2 + hcl
                    ps, B_ps = (psC, B_C) if hc % 2 == 0 else (psD, B_D)
                    for ic in range(IC):
                        ap, bb = dn_tiles[(e, hcg, ic)]
                        lhs = ap[:, hcl * 128:(hcl + 1) * 128]
                        for tn in range(2):
                            sl = slice(tn * 512, (tn + 1) * 512)
                            b.emit("pe", mm_fn(ps[:, sl], lhs,
                                               actw[par][ic][:, sl],
                                               ic == 0, ic == IC - 1),
                                   reads=[bb, B_actw[par][ic]], writes=[B_ps])
                    b.emit("dve", (lambda hh, ps_: lambda v:
                                   v.scalar_tensor_tensor(
                                       out=X[hh], in0=ps_[:, :], scalar=0.5,
                                       in1=X[hh], op0=ALU.mult, op1=ALU.add))(
                        hc, ps), reads=[B_ps, B_X[hc]], writes=[B_X[hc]])
                if hcg % 2 == 1:
                    dn_gen(4 * e + hcg // 2 + 2)

        for e in range(E):
            if e + 1 < E:
                mw_stream(e + 1)
            if e + 2 < E and first_call:
                emit_casts(e + 2)
            if first_call and e == 24 and stage >= 2:
                emit_qkv_casts()
            gu_block(e)
            if e > 0:
                dn_block(e - 1)
        dn_block(E - 1)

    def ffn_phase(tag, first_call):
        AB.mark()
        AR.mark()
        xn = [AB.alloc(T) for _ in range(HC)]
        B_xn = [b.buf(f"xn{tag}_{hc}") for hc in range(HC)]
        xh = AB.alloc(T)
        B_xh = b.buf(f"xh{tag}")
        sq = AB.alloc(T)
        B_sq = b.buf(f"sq{tag}")
        gu_half = [AB.alloc(HC * 384) for _ in range(2)]
        B_guh = [[b.buf(f"gur{tag}_{h}_{i}") for i in range(HC)]
                 for h in range(2)]
        dn_half = [AB.alloc(IC * 512) for _ in range(2)]
        B_dnh = [[b.buf(f"dnr{tag}_{h}_{i}") for i in range(IC)]
                 for h in range(2)]
        actw = [[AB.alloc(T) for _ in range(IC)] for _ in range(2)]
        B_actw = [[b.buf(f"actw{tag}_{p}_{ic}") for ic in range(IC)]
                  for p in range(2)]
        AR.mark()
        t1 = AR.alloc(T)[:, :]
        B_t1 = b.buf(f"t1{tag}")
        Ra = AR.alloc(T)[0:1, :]
        Rb = AR.alloc(T)[0:1, :]
        Rc = AR.alloc(T)[0:1, :]
        B_rows = b.buf(f"rows{tag}")
        rbb = AR.alloc(T)
        B_rb = b.buf(f"rb{tag}")
        mrb = AR.alloc(T)
        B_mrb = b.buf(f"mrb{tag}")
        ltT = AR.alloc(T)[0:E, :]
        B_ltT = b.buf(f"ltT{tag}")
        mwT = AR.alloc(T)[0:E, :]
        B_mwT = b.buf(f"mwT{tag}")

        emit_ln(xn, B_xn, xh, B_xh, sq, B_sq, t1, B_t1, Ra, Rb, Rc, B_rows,
                rbb, B_rb, mrb, B_mrb, psC, B_C, psD, B_D)
        emit_router(xn, B_xn, psC, B_C, psD, B_D, ltT, B_ltT, mwT, B_mwT)
        fence = b.fence_deps()
        AR.reset()
        AR.mark()
        mwb = [AR.alloc(T) for _ in range(2)]
        B_mwb = [b.buf(f"mwb{tag}_{i}") for i in range(2)]
        g_s = AR.alloc(T)
        B_gs = b.buf(f"g_s{tag}")
        prod = AR.alloc(T)
        B_prod = b.buf(f"prod{tag}")
        uf32 = [AR.alloc(H) for _ in range(2)]
        B_uf32 = [b.buf(f"uf32{tag}_{i}") for i in range(2)]
        emit_ffn(xn, B_xn, gu_half, B_guh, dn_half, B_dnh, actw, B_actw,
                 mwb, B_mwb, g_s, B_gs, prod, B_prod, uf32, B_uf32,
                 first_call, fence)
        AR.reset()
        AB.reset()
        AR.reset()

    ffn_phase("1", True)

    if stage >= 2:
        b.barrier()
        AB.mark()
        AR.mark()
        qT = [AB.alloc(T) for _ in range(HC)]
        B_qT = [b.buf(f"qT{i}") for i in range(HC)]

        AB.mark()
        AR.mark()
        xn2 = [AB.alloc(T) for _ in range(HC)]
        B_xn2 = [b.buf(f"xn2_{hc}") for hc in range(HC)]
        xh2 = AB.alloc(T)
        B_xh2 = b.buf("xh2")
        sq2 = AB.alloc(T)
        B_sq2 = b.buf("sq2")
        qkv_ring = Ring([(AB.alloc(512), b.buf(f"qkr{i}")) for i in range(16)])
        kv_stage = [AB.alloc(512) for _ in range(2)]
        B_kvs = [b.buf(f"kvs{i}") for i in range(2)]
        t12 = AR.alloc(T)[:, :]
        B_t12 = b.buf("t12")
        Ra2 = AR.alloc(T)[0:1, :]
        Rb2 = AR.alloc(T)[0:1, :]
        Rc2 = AR.alloc(T)[0:1, :]
        B_rows2 = b.buf("rows2")
        rbb2 = AR.alloc(T)
        B_rb2 = b.buf("rb2")
        mrb2 = AR.alloc(T)
        B_mrb2 = b.buf("mrb2")

        emit_ln(xn2, B_xn2, xh2, B_xh2, sq2, B_sq2, t12, B_t12,
                Ra2, Rb2, Rc2, B_rows2, rbb2, B_rb2, mrb2, B_mrb2,
                psA, B_A, psB, B_B)

        qkv_tiles = {}
        og_order = [4, 5, 6, 7, 8, 9, 10, 11, 0, 1, 2, 3]
        halves = [(og, h) for og in og_order for h in range(2)]
        half_i = [0]

        def qkv_half_gen():
            if half_i[0] >= len(halves):
                return
            og, h = halves[half_i[0]]
            half_i[0] += 1
            for hcl in range(8):
                hc = h * 8 + hcl
                ap, bb = qkv_ring.next()
                src = wqkv_b[og * 512:(og + 1) * 512, hc * 128:(hc + 1) * 128]
                b.emit(b.q("qkv"), dma_fn(ap, src, tr=True),
                       reads=[B_wqkvb[og]], writes=[bb], dma=True)
                qkv_tiles[(og, hc)] = (ap, bb)

        qkv_half_gen()
        qkv_half_gen()

        for oi, og in enumerate(og_order):
            for ocl in range(4):
                oc = og * 4 + ocl
                ps, B_ps = (psA, B_A) if oc % 2 == 0 else (psB, B_B)
                for hc in range(HC):
                    ap, bb = qkv_tiles[(og, hc)]
                    lhs = ap[:, ocl * 128:(ocl + 1) * 128]
                    for tn in range(2):
                        sl = slice(tn * 512, (tn + 1) * 512)
                        b.emit("pe", mm_fn(ps[:, sl], lhs, xn2[hc][:, sl],
                                           hc == 0, hc == HC - 1),
                               reads=[bb, B_xn2[hc]], writes=[B_ps])
                if oc < 16:
                    b.emit("dve", (lambda o, ps_, c: lambda v: v.tensor_scalar(
                        out=o, in0=ps_[:, :], scalar1=bq_cols[:, c:c + 1],
                        scalar2=INV_SQRT_H, op0=ALU.add, op1=ALU.mult))(
                        qT[oc], ps, oc),
                        reads=[B_ps, B_bq], writes=[B_qT[oc]])
                else:
                    dstd = kv_in
                    dstb = B_kvin
                    r0 = (oc - 16) * 128 if oc < 32 else H + (oc - 32) * 128
                    for tn in range(2):
                        sl = slice(tn * 512, (tn + 1) * 512)
                        st, stb = kv_stage[tn], B_kvs[tn]
                        b.emit("dve", (lambda o, ps_, c, s_: lambda v:
                                       v.tensor_scalar(
                                           out=o, in0=ps_[:, s_],
                                           scalar1=bq_cols[:, c:c + 1],
                                           scalar2=None, op0=ALU.add))(
                            st, ps, oc, sl),
                            reads=[B_ps, B_bq], writes=[stb])
                        b.emit("pool", (lambda d, r, s_, st_: lambda s:
                                        s.dma_start(out=d[r:r + 128, s_],
                                                    in_=st_))(dstd, r0, sl, st),
                               reads=[stb], writes=[dstb], dma=True)
            qkv_half_gen()
            qkv_half_gen()
            if og == 11 and ATTN_UNTIL >= 2:
                b.emit("pool", lambda g: g.collective_compute(
                    "AllGather", ALU.bypass,
                    replica_groups=[[0, 1, 2, 3], [4, 5, 6, 7]],
                    ins=[kv_in.opt()], outs=[kv_out.opt()]),
                    reads=[B_kvin], writes=[B_kvout], cc=True)

        if ATTN_UNTIL >= 3:
            vstage = [AB.alloc(H) for _ in range(2)]
            B_vst = [b.buf(f"vst{i}") for i in range(2)]
            for kc in range(32):
                rk, tl = divmod(kc, 8)
                st, stb = vstage[kc % 2], B_vst[kc % 2]
                b.emit("sp", dma_fn(
                    st, kv_out[rk * 2 * H + H: (rk + 1) * 2 * H,
                               tl * 128:(tl + 1) * 128], tr=True),
                    reads=[B_kvout], writes=[stb], dma=True)
                b.emit("pool", (lambda kc_, st_: lambda s: s.dma_start(
                    out=v_tm[kc_ * 128:(kc_ + 1) * 128, :], in_=st_))(kc, st),
                    reads=[stb], writes=[B_vtm], dma=True)

        AB.reset()
        AR.reset()
        b.barrier()

        AB.mark()
        AR.mark()
        ctxT_all = [AB.alloc(T) for _ in range(HC)]
        B_ctxT = [b.buf(f"ctxT{hc}") for hc in range(HC)]
        AB.mark()
        PT = AB.alloc(4096)
        B_PT = [b.buf(f"PT{i}") for i in range(4)]
        kt_ring = Ring([(AB.alloc(2 * T), b.buf(f"ktr{i}")) for i in range(2)])
        v_ring = Ring([(AB.alloc(2 * H), b.buf(f"vr{i}")) for i in range(2)])
        S_sb = AR.alloc(4096)
        B_S = b.buf("S")
        e_f32 = [AR.alloc(T) for _ in range(2)]
        B_e = [b.buf(f"ef{i}") for i in range(2)]
        ctx_f32 = AR.alloc(H)
        B_ctx = b.buf("ctx")
        nmax2 = AR.alloc(1)
        B_nm2 = b.buf("nm2")
        zacc = AR.alloc(4)
        B_za = b.buf("za")
        rz2 = AR.alloc(1)
        B_rz2 = b.buf("rz2")

        kt_tiles = {}
        kt_gi = [0]

        def kt_gen():
            g = kt_gi[0]
            kt_gi[0] += 1
            if g >= 8 * 4 * 8:
                return
            qt, r = divmod(g, 32)
            kb, ocg = divmod(r, 8)
            ap, bb = kt_ring.next()
            rows = kv_out[kb * 2 * H + ocg * 256:
                          kb * 2 * H + (ocg + 1) * 256, :]
            b.emit("sp", dma_fn(ap.rearrange("p (g t) -> p g t", g=2),
                                rows.rearrange("(g p) t -> p g t", p=128)),
                   reads=[B_kvout], writes=[bb], dma=True)
            for ocl in range(2):
                kt_tiles[(qt, kb, ocg * 2 + ocl)] = (
                    ap[:, ocl * T:(ocl + 1) * T], bb)

        v_tiles = {}
        v_gi = [0]

        def v_gen():
            g = v_gi[0]
            v_gi[0] += 1
            if g >= 8 * 16:
                return
            qt, kcp = divmod(g, 16)
            ap, bb = v_ring.next()
            rows = v_tm[kcp * 256:(kcp + 1) * 256, :]
            b.emit("sp", dma_fn(ap.rearrange("p (g h) -> p g h", g=2),
                                rows.rearrange("(g p) h -> p g h", p=128)),
                   reads=[B_vtm], writes=[bb], dma=True)
            for kcl in range(2):
                v_tiles[(qt, kcp * 2 + kcl)] = (
                    ap[:, kcl * H:(kcl + 1) * H], bb)

        if ATTN_UNTIL >= 3:
            kt_gen()
            kt_gen()
            v_gen()
            v_gen()

        for qt in range(8 if ATTN_UNTIL >= 3 else 0):
            for kb in range(4):
                ps, B_ps = (psA, B_A) if kb % 2 == 0 else (psB, B_B)
                for ocg in range(8):
                    for ocl in range(2):
                        oc = ocg * 2 + ocl
                        ap, bb = kt_tiles[(qt, kb, oc)]
                        lhs = qT[oc][:, qt * 128:(qt + 1) * 128]
                        for kn in range(2):
                            sl = slice(kn * 512, (kn + 1) * 512)
                            b.emit("pe", mm_fn(ps[:, sl], lhs, ap[:, sl],
                                               oc == 0, oc == HC - 1),
                                   reads=[bb, B_qT[oc]], writes=[B_ps])
                    kt_gen()
                b.emit("dve", (lambda kb_, ps_: lambda v: v.tensor_copy(
                    S_sb[:, kb_ * T:(kb_ + 1) * T], ps_[:, :]))(kb, ps),
                    reads=[B_ps], writes=[B_S])
            b.emit("dve", lambda v: v.reduce_max(nmax2, S_sb, axis=AX.X,
                                                 negate=True),
                   reads=[B_S], writes=[B_nm2])
            for kb in range(4):
                et, eb = e_f32[kb % 2], B_e[kb % 2]
                b.emit("act", (lambda o, kb_: lambda s: s.activation(
                    o, S_sb[:, kb_ * T:(kb_ + 1) * T], AF.Exp, bias=nmax2))(
                    et, kb), reads=[B_S, B_nm2], writes=[eb])
                b.emit("dve", (lambda i, kb_: lambda v: v.reduce_sum(
                    zacc[:, kb_:kb_ + 1], i, axis=AX.X))(et, kb),
                    reads=[eb], writes=[B_za])
                for blk in range(8):
                    kc = kb * 8 + blk
                    pd, B_pd = ((psC[:, 0:512], B_Ca) if kc % 2 == 0
                                else (psC[:, 512:1024], B_Cb))
                    b.emit("pe", tr_fn(pd[:, 0:128],
                                       et[:, blk * 128:(blk + 1) * 128],
                                       ident[:, :]),
                           reads=[eb, B_ident], writes=[B_pd])
                    b.emit("dve", (lambda kc_, pd_: lambda v: v.tensor_copy(
                        PT[:, kc_ * 128:(kc_ + 1) * 128], pd_[:, 0:128]))(
                        kc, pd), reads=[B_pd], writes=[B_PT[kb]])
            b.emit("dve", lambda v: v.reduce_sum(rz2, zacc, axis=AX.X),
                   reads=[B_za], writes=[B_rz2])
            b.emit("dve", lambda v: v.reciprocal(rz2, rz2),
                   reads=[B_rz2], writes=[B_rz2])
            for kc in range(32):
                ap, bb = v_tiles[(qt, kc)]
                lhs = PT[:, kc * 128:(kc + 1) * 128]
                for hn in range(2):
                    ps, B_ps = (psA, B_A) if hn == 0 else (psB, B_B)
                    for tn in range(2):
                        b.emit("pe", mm_fn(
                            ps[:, tn * 512:(tn + 1) * 512], lhs,
                            ap[:, hn * 1024 + tn * 512:
                               hn * 1024 + (tn + 1) * 512],
                            kc == 0, kc == 31),
                            reads=[bb, B_PT[kc // 8]], writes=[B_ps])
                if kc % 2 == 1:
                    v_gen()
            for hn in range(2):
                ps, B_ps = (psA, B_A) if hn == 0 else (psB, B_B)
                b.emit("dve", (lambda hn_, ps_: lambda v: v.tensor_scalar(
                    out=ctx_f32[:, hn_ * 1024:(hn_ + 1) * 1024],
                    in0=ps_[:, :], scalar1=rz2, scalar2=None, op0=ALU.mult))(
                    hn, ps), reads=[B_ps, B_rz2], writes=[B_ctx])
            for hc in range(HC):
                pd, B_pd = ((psD[:, 0:512], B_Da) if hc % 2 == 0
                            else (psD[:, 512:1024], B_Db))
                b.emit("pe", tr_fn(pd[:, 0:128],
                                   ctx_f32[:, hc * 128:(hc + 1) * 128],
                                   ident[:, :]),
                       reads=[B_ctx, B_ident], writes=[B_pd])
                b.emit("dve", (lambda hc_, qt_, pd_: lambda v: v.tensor_copy(
                    ctxT_all[hc_][:, qt_ * 128:(qt_ + 1) * 128], pd_[:, 0:128]))(
                    hc, qt, pd), reads=[B_pd], writes=[B_ctxT[hc]])

        AB.reset()
        wo_ring = Ring([(AB.alloc(512), b.buf(f"wor{i}")) for i in range(24)])
        wo_fence = b.fence_deps()
        wo_tiles = {}
        wo_hi = [0]

        def wo_half_gen():
            if wo_hi[0] >= 8:
                return
            og, h = divmod(wo_hi[0], 2)
            first = wo_hi[0] <= 1
            wo_hi[0] += 1
            for hcl in range(8):
                hc = h * 8 + hcl
                ap, bb = wo_ring.next()
                src = wout_b[og * 512:(og + 1) * 512, hc * 128:(hc + 1) * 128]
                b.emit(b.q("wo"), dma_fn(ap, src, tr=True),
                       reads=[B_woutb[og]], writes=[bb], dma=True,
                       deps=(wo_fence if first else ()))
                wo_tiles[(og, hc)] = (ap, bb)

        if ATTN_UNTIL >= 4:
            wo_half_gen()
            wo_half_gen()
            wo_half_gen()
        for og in range(4 if ATTN_UNTIL >= 4 else 0):
            for ocl in range(4):
                oc = og * 4 + ocl
                ps, B_ps = (psA, B_A) if oc % 2 == 0 else (psB, B_B)
                for hc in range(HC):
                    ap, bb = wo_tiles[(og, hc)]
                    lhs = ap[:, ocl * 128:(ocl + 1) * 128]
                    for tn in range(2):
                        sl = slice(tn * 512, (tn + 1) * 512)
                        b.emit("pe", mm_fn(ps[:, sl], lhs,
                                           ctxT_all[hc][:, sl],
                                           hc == 0, hc == HC - 1),
                               reads=[bb, B_ctxT[hc]], writes=[B_ps])
                b.emit("dve", (lambda oc_, ps_: lambda v:
                               v.scalar_tensor_tensor(
                                   out=X[oc_], in0=ps_[:, :],
                                   scalar=bo_cols[:, oc_:oc_ + 1],
                                   in1=X[oc_], op0=ALU.add, op1=ALU.add))(
                    oc, ps), reads=[B_ps, B_X[oc], B_bo], writes=[B_X[oc]])
            wo_half_gen()
            wo_half_gen()
        AB.reset()
        AR.reset()
        AB.reset()
        AR.reset()

    if stage >= 3:
        b.barrier()
        ffn_phase("3", False)

    out_ops = []
    B_outs = [b.buf(f"outd{hc}") for hc in range(HC)]
    for hc in range(HC):
        op = b.emit("sp", dma_fn(out[hc * 128:(hc + 1) * 128, :], X[hc]),
                    reads=[B_X[hc]], writes=[B_outs[hc]], dma=True)
        out_ops.append(op)
    b.emit("pool", lambda g: g.engine_nop(), deps=out_ops)

    b.run_block()
    return nc


class Ring:
    def __init__(self, items):
        self.items = items
        self.i = 0

    def next(self):
        r = self.items[self.i % len(self.items)]
        self.i += 1
        return r


_NC_CACHE = {}


def kernel(**inputs):
    x = np.asarray(inputs["L0_attn_hidden_states"], dtype=np.float32)
    xf = np.ascontiguousarray(x.reshape(8192, 2048))
    stage = DEBUG_STAGE
    if stage not in _NC_CACHE:
        _NC_CACHE[stage] = build_nc(stage)
    nc = _NC_CACHE[stage]
    base = {
        "wqkv": np.asarray(inputs["L0_attn_w_qkv"], dtype=np.float32),
        "bqkv": np.asarray(inputs["L0_attn_b_qkv"], dtype=np.float32),
        "wout": np.asarray(inputs["L0_attn_w_out"], dtype=np.float32),
        "bout": np.asarray(inputs["L0_attn_b_out"], dtype=np.float32),
        "gw": np.asarray(inputs["L0_ffn_gate_w"], dtype=np.float32),
        "wg": np.asarray(inputs["L0_ffn_w_gate_proj"], dtype=np.float32),
        "wu": np.asarray(inputs["L0_ffn_w_up_proj"], dtype=np.float32),
        "wd": np.asarray(inputs["L0_ffn_w_down_proj"], dtype=np.float32),
    }
    in_maps = []
    for c in range(8):
        m = dict(base)
        m["x"] = np.ascontiguousarray(xf[c * 1024:(c + 1) * 1024, :].T)
        in_maps.append(m)
    res = run_bass_kernel_spmd(nc, in_maps, core_ids=list(range(8)))
    parts = [np.ascontiguousarray(res.results[c]["out"].T) for c in range(8)]
    return np.concatenate(parts, axis=0).reshape(2, 4096, 2048)


# version 09
# speedup vs baseline: 1.1252x; 1.1252x over previous
import os

os.environ.setdefault("NEURON_SCRATCHPAD_PAGE_SIZE", "1024")

import numpy as np
import concourse.bass as bass
import concourse.mybir as mybir
from concourse.bass_utils import run_bass_kernel_spmd

F32 = mybir.dt.float32
BF16 = mybir.dt.bfloat16
AF = mybir.ActivationFunctionType
ALU = mybir.AluOpType
AX = mybir.AxisListType

H = 2048
HC = H // 128
I = 768
IC = I // 128
E = 32
T = 1024
GRP = 4
INV_SQRT_H = float(1.0 / np.sqrt(np.float32(H)))
LN_EPS = 1e-5

DEBUG_STAGE = 3
ATTN_UNTIL = 4
Q_STREAMS = ()


class Op:
    __slots__ = ("eng", "fn", "deps", "inc", "amt", "sem", "val", "waits",
                 "cc")

    def __init__(self, eng, fn, amt):
        self.eng = eng
        self.fn = fn
        self.amt = amt
        self.deps = ()
        self.inc = False
        self.sem = None
        self.val = 0
        self.waits = ()
        self.cc = False


class Buf:
    __slots__ = ("name", "w", "r")

    def __init__(self, name):
        self.name = name
        self.w = None
        self.r = {}


class Bld:
    ENGS = ("pe", "act", "dve", "pool", "sp")

    def __init__(self, nc):
        self.nc = nc
        self.ops = []
        self.pending = {}
        self.bufs = []
        self.last_eng = {}
        self.qi = 0

    def q(self, tag=None):
        if tag not in Q_STREAMS:
            return "sp"
        self.qi += 1
        return "sp" if self.qi % 2 == 0 else "act"

    def buf(self, name):
        b = Buf(name)
        self.bufs.append(b)
        return b

    def emit(self, eng, fn, reads=(), writes=(), deps=(), dma=False, cc=False):
        op = Op(eng, fn, 16 if dma else 1)
        op.cc = cc
        dd = {}

        def add(d):
            if d is None or d is op:
                return
            if d.eng == "pe" and eng == "pe":
                return
            dd[id(d)] = d

        for b in reads:
            add(b.w)
        for b in writes:
            add(b.w)
            for d in b.r.values():
                add(d)
        for d in deps:
            add(d)
        for d in self.pending.pop(eng, ()):
            add(d)
        op.deps = list(dd.values())
        for b in reads:
            b.r[eng] = op
        for b in writes:
            b.w = op
            b.r = {}
        self.ops.append(op)
        self.last_eng[eng] = op
        return op

    def fence_deps(self):
        return [o for o in self.last_eng.values()]

    def barrier(self):
        for e in self.ENGS:
            prev = list(self.pending.get(e, ()))
            self.pending[e] = prev + [op for k, op in self.last_eng.items()
                                      if k != e]
        for b in self.bufs:
            b.w = None
            b.r = {}

    def finalize(self):
        for op in self.ops:
            for d in op.deps:
                d.inc = True
        sems = {}
        tot = {}
        cnt = {}
        NS = 8
        dma_pools = {}
        for op in self.ops:
            if op.amt == 16:
                op.inc = True
            if not op.inc:
                continue
            if op.cc:
                e = "cc"
            elif op.amt == 16:
                key = op.eng + "_d"
                if key not in dma_pools:
                    dma_pools[key] = [
                        [self.nc.ctx.enter_context(
                            self.nc.semaphore(f"s_{key}_{i}"))
                         for i in range(NS)], 0]
                pool, c = dma_pools[key]
                op.sem = pool[c % NS]
                op.val = 16 * (c // NS + 1)
                dma_pools[key][1] = c + 1
                continue
            else:
                e = op.eng + "_c"
            if e not in sems or tot[e] + op.amt > 48000:
                cnt[e] = cnt.get(e, 0) + 1
                sems[e] = self.nc.ctx.enter_context(
                    self.nc.semaphore(f"s_{e}_{cnt[e]}"))
                tot[e] = 0
            tot[e] += op.amt
            op.sem = sems[e]
            op.val = tot[e]
        waited = {e: {} for e in self.ENGS}
        for op in self.ops:
            w = {}
            for d in op.deps:
                key = id(d.sem)
                if w.get(key, (None, 0))[1] < d.val:
                    w[key] = (d.sem, d.val)
            ws = []
            for key, (sem, val) in w.items():
                if waited[op.eng].get(key, 0) >= val:
                    continue
                waited[op.eng][key] = val
                ws.append((sem, val))
            op.waits = ws

    def run_block(self):
        self.finalize()
        per = {e: [o for o in self.ops if o.eng == e] for e in self.ENGS}
        nc = self.nc
        with nc.Block() as block:
            def mk(ops):
                def body(eng):
                    for op in ops:
                        for sem, val in op.waits:
                            eng.wait_ge(sem, val)
                        ins = op.fn(eng)
                        if op.inc:
                            ins.then_inc(op.sem, op.amt)
                return body
            block.tensor(mk(per["pe"]))
            block.scalar(mk(per["act"]))
            block.vector(mk(per["dve"]))
            block.gpsimd(mk(per["pool"]))
            block.sync(mk(per["sp"]))


class Arena:
    def __init__(self, nc, name, total_bytes, dtype):
        esz = 2 if dtype == BF16 else 4
        self.n = total_bytes // esz
        self.t = nc.alloc_sbuf_tensor(name, [128, self.n], dtype).ap()
        self.off = 0
        self.marks = []

    def alloc(self, width):
        a = self.t[:, self.off:self.off + width]
        self.off += width
        assert self.off <= self.n, f"arena overflow {self.off} > {self.n}"
        return a

    def mark(self):
        self.marks.append(self.off)

    def reset(self):
        self.off = self.marks.pop()


def mm_fn(o, l, r, st, sp):
    return lambda t: t.matmul(o, l, r, start=st, stop=sp)


def dma_fn(o, i, tr=False):
    if tr:
        return lambda s: s.dma_start(out=o, in_=i, transpose=True)
    return lambda s: s.dma_start(out=o, in_=i)


def tr_fn(o, i, ident):
    return lambda t: t.transpose(o, i, ident)


def copy_fn(o, i):
    return lambda v: v.tensor_copy(o, i)


def build_nc(stage=3):
    nc = bass.Bass("TRN2", target_bir_lowering=False)

    x_in = nc.dram_tensor("x", [H, T], F32, kind="ExternalInput").ap()
    wqkv = nc.dram_tensor("wqkv", [3 * H, H], F32, kind="ExternalInput").ap()
    bqkv = nc.dram_tensor("bqkv", [3 * H], F32, kind="ExternalInput").ap()
    wout = nc.dram_tensor("wout", [H, H], F32, kind="ExternalInput").ap()
    bout = nc.dram_tensor("bout", [H], F32, kind="ExternalInput").ap()
    gw = nc.dram_tensor("gw", [E, H], F32, kind="ExternalInput").ap()
    wg = nc.dram_tensor("wg", [E, I, H], F32, kind="ExternalInput").ap()
    wu = nc.dram_tensor("wu", [E, I, H], F32, kind="ExternalInput").ap()
    wd = nc.dram_tensor("wd", [E, H, I], F32, kind="ExternalInput").ap()
    out = nc.dram_tensor("out", [H, T], F32, kind="ExternalOutput").ap()

    kv_in = nc.dram_tensor("kv_in", [2 * H, T], BF16, kind="Internal").ap()
    kv_out = nc.dram_tensor("kv_out", [GRP * 2 * H, T], BF16,
                            kind="Internal").ap()
    wqkv_b = nc.dram_tensor("wqkv_b", [3 * H, H], BF16, kind="Internal").ap()
    wout_b = nc.dram_tensor("wout_b", [H, H], BF16, kind="Internal").ap()
    gw_b = nc.dram_tensor("gw_b", [E, H], BF16, kind="Internal").ap()
    wg_b = nc.dram_tensor("wg_b", [E, I, H], BF16, kind="Internal").ap()
    wu_b = nc.dram_tensor("wu_b", [E, I, H], BF16, kind="Internal").ap()
    wd_b = nc.dram_tensor("wd_b", [E, H, I], BF16, kind="Internal").ap()
    wguT_c = nc.dram_tensor("wguT_c", [2, E, 2, 128, HC * 384], BF16,
                            kind="Internal").ap()
    wdT_c = nc.dram_tensor("wdT_c", [E, 4, 128, IC * 512], BF16,
                           kind="Internal").ap()
    v_tm = nc.dram_tensor("v_tm", [GRP * T, H], BF16, kind="Internal").ap()
    mw_dram = nc.dram_tensor("mw_dram", [E, T], F32, kind="Internal").ap()
    ln_rows = nc.dram_tensor("ln_rows", [2, T], F32, kind="Internal").ap()

    b = Bld(nc)

    B_wgb = [b.buf(f"wgb{e}") for e in range(E)]
    B_wub = [b.buf(f"wub{e}") for e in range(E)]
    B_wdb = [b.buf(f"wdb{e}") for e in range(E)]
    B_wqkvb = [b.buf(f"wqkvb{i}") for i in range(12)]
    B_woutb = [b.buf(f"woutb{i}") for i in range(4)]
    B_gwb = b.buf("gwb")
    B_wguT = [[b.buf(f"wguT{m}_{e}") for e in range(E)] for m in range(2)]
    B_wdT = [b.buf(f"wdT{e}") for e in range(E)]
    B_vtm = b.buf("vtm")
    B_mwd = b.buf("mwd")
    B_lnr = b.buf("lnr")
    B_kvin = b.buf("kvin")
    B_kvout = b.buf("kvout")
    B_out = b.buf("outd")

    psA = nc.ctx.enter_context(nc.psum_tensor("psA", [128, 1024], F32)).ap()
    psB = nc.ctx.enter_context(nc.psum_tensor("psB", [128, 1024], F32)).ap()
    psC = nc.ctx.enter_context(nc.psum_tensor("psC", [128, 1024], F32)).ap()
    psD = nc.ctx.enter_context(nc.psum_tensor("psD", [128, 1024], F32)).ap()
    B_A = b.buf("psA")
    B_B = b.buf("psB")
    B_C = b.buf("psC")
    B_D = b.buf("psD")
    B_Ca = b.buf("psCa")
    B_Cb = b.buf("psCb")
    B_Da = b.buf("psDa")
    B_Db = b.buf("psDb")

    def sb(name, shape, dt):
        return nc.alloc_sbuf_tensor(name, shape, dt).ap()

    X = [sb(f"X{hc}", [128, T], F32) for hc in range(HC)]
    B_X = [b.buf(f"X{hc}") for hc in range(HC)]
    ident = sb("ident", [128, 128], F32)
    identb = sb("identb", [128, 128], BF16)
    ones_col = sb("ones_col", [128, 1], BF16)
    bq_cols = sb("bq_cols", [128, 48], F32)
    bo_cols = sb("bo_cols", [128, 16], F32)
    l_sb = sb("l_sb", [128, E], F32)
    mw_sb = sb("mw_sb", [128, E], F32)
    gwT = [sb(f"gwT{hc}", [128, E], BF16) for hc in range(HC)]
    B_gwT = [b.buf(f"gwT{hc}") for hc in range(HC)]
    topk8 = sb("topk8", [128, 8], F32)
    zap = sb("zap", [128, E], F32)
    selm = sb("selm", [128, E], F32)
    expl = sb("expl", [128, E], F32)
    nmax = sb("nmax", [128, 1], F32)
    zsum = sb("zsum", [128, 1], F32)
    rz1 = sb("rz1", [128, 1], F32)
    eps_c = sb("eps_c", [128, 1], F32)
    B_tk = b.buf("topk")
    B_ident = b.buf("ident")
    B_identb = b.buf("identb")
    B_ones = b.buf("ones")
    B_bq = b.buf("bq")
    B_bo = b.buf("bo")

    AB = Arena(nc, "arena_bf", 109 * 1024, BF16)
    AR = Arena(nc, "arena_f32", 30 * 1024, F32)

    for hc in range(HC):
        b.emit("pool", dma_fn(X[hc], x_in[hc * 128:(hc + 1) * 128, :]),
               writes=[B_X[hc]], dma=True)
    b.emit("pool", lambda g: g.memset(ident[:, :], 0.0), writes=[B_ident])
    b.emit("pool", lambda g: g.affine_select(
        out=ident[:, :], in_=ident[:, :], compare_op=ALU.not_equal,
        fill=1.0, base=0, pattern=[[-1, 128]], channel_multiplier=1),
        reads=[B_ident], writes=[B_ident])
    b.emit("dve", copy_fn(identb[:, :], ident[:, :]),
           reads=[B_ident], writes=[B_identb])
    b.emit("pool", lambda g: g.memset(ones_col[:, :], 1.0), writes=[B_ones])
    b.emit("pool", lambda g: g.memset(eps_c[:, :], LN_EPS), writes=[B_ones])
    b.emit("pool", dma_fn(gw_b[:, :], gw[:, :]), writes=[B_gwb], dma=True)

    cast_done = [False] * E

    def emit_casts(e):
        if cast_done[e]:
            return
        cast_done[e] = True
        b.emit("pool", dma_fn(wg_b[e, :, :], wg[e, :, :]),
               writes=[B_wgb[e]], dma=True)
        b.emit("pool", dma_fn(wu_b[e, :, :], wu[e, :, :]),
               writes=[B_wub[e]], dma=True)
        b.emit("pool", dma_fn(wd_b[e, :, :], wd[e, :, :]),
               writes=[B_wdb[e]], dma=True)

    emit_casts(0)
    emit_casts(1)

    for j in range(48):
        b.emit("pool", dma_fn(bq_cols[:, j:j + 1],
                              bqkv[j * 128:(j + 1) * 128, None]),
               writes=[B_bq], dma=True)
    for j in range(16):
        b.emit("pool", dma_fn(bo_cols[:, j:j + 1],
                              bout[j * 128:(j + 1) * 128, None]),
               writes=[B_bo], dma=True)

    qkv_cast_emitted = [False]

    def emit_qkv_casts():
        if qkv_cast_emitted[0]:
            return
        qkv_cast_emitted[0] = True
        for i in range(12):
            b.emit("pool", dma_fn(wqkv_b[i * 512:(i + 1) * 512, :],
                                  wqkv[i * 512:(i + 1) * 512, :]),
                   writes=[B_wqkvb[i]], dma=True)
        for i in range(4):
            b.emit("pool", dma_fn(wout_b[i * 512:(i + 1) * 512, :],
                                  wout[i * 512:(i + 1) * 512, :]),
                   writes=[B_woutb[i]], dma=True)

    for hc in range(HC):
        b.emit(b.q("gw"), dma_fn(gwT[hc][:, :], gw_b[:, hc * 128:(hc + 1) * 128],
                            tr=True),
               reads=[B_gwb], writes=[B_gwT[hc]], dma=True)

    def emit_ln(xn, B_xn, xh, B_xh, sq, B_sq, t1, B_t1, Ra, Rb, Rc, B_rows,
                rbb, B_rb, mrb, B_mrb, ps_x, B_px, ps_sq, B_psq, fence=()):
        first = dict(deps=list(fence))
        for hc in range(HC):
            b.emit("dve", copy_fn(xh, X[hc]), reads=[B_X[hc]], writes=[B_xh],
                   **(first if hc == 0 else {}))
            b.emit("act", (lambda o, i: lambda s: s.activation(o, i, AF.Square))(
                sq, xh), reads=[B_xh], writes=[B_sq])
            for tn in range(2):
                sl = slice(tn * 512, (tn + 1) * 512)
                b.emit("pe", mm_fn(ps_x[0:1, sl], ones_col[:, :], xh[:, sl],
                                   hc == 0, hc == HC - 1),
                       reads=[B_xh, B_ones], writes=[B_px])
                b.emit("pe", mm_fn(ps_sq[0:1, sl], ones_col[:, :], sq[:, sl],
                                   hc == 0, hc == HC - 1),
                       reads=[B_sq, B_ones], writes=[B_psq])
        b.emit("dve", lambda v: v.tensor_scalar(
            out=Ra, in0=ps_x[0:1, :], scalar1=1.0 / H, scalar2=None,
            op0=ALU.mult), reads=[B_px], writes=[B_rows], deps=list(fence))
        b.emit("dve", lambda v: v.tensor_scalar(
            out=Rb, in0=ps_sq[0:1, :], scalar1=1.0 / H, scalar2=None,
            op0=ALU.mult), reads=[B_psq], writes=[B_rows])
        b.emit("dve", lambda v: v.tensor_mul(Rc, Ra, Ra),
               reads=[B_rows], writes=[B_rows])
        b.emit("dve", lambda v: v.tensor_sub(Rb, Rb, Rc),
               reads=[B_rows], writes=[B_rows])
        b.emit("act", lambda s: s.activation(Rc, Rb, AF.Sqrt,
                                             bias=eps_c[0:1, :]),
               reads=[B_rows, B_ones], writes=[B_rows])
        b.emit("dve", lambda v: v.reciprocal(Rb, Rc),
               reads=[B_rows], writes=[B_rows])
        b.emit("dve", lambda v: v.tensor_mul(Rc, Ra, Rb),
               reads=[B_rows], writes=[B_rows])
        b.emit("pool", dma_fn(ln_rows[0:1, :], Rb),
               reads=[B_rows], writes=[B_lnr], dma=True)
        b.emit("pool", dma_fn(ln_rows[1:2, :], Rc),
               reads=[B_rows], writes=[B_lnr], dma=True)
        b.emit("pool", dma_fn(rbb, ln_rows[0:1, :].partition_broadcast(128)),
               reads=[B_lnr], writes=[B_rb], dma=True, deps=list(fence))
        b.emit("pool", dma_fn(mrb, ln_rows[1:2, :].partition_broadcast(128)),
               reads=[B_lnr], writes=[B_mrb], dma=True)
        for hc in range(HC):
            b.emit("dve", (lambda o, i: lambda v: v.tensor_mul(o, i, rbb))(
                t1, X[hc]), reads=[B_X[hc], B_rb], writes=[B_t1],
                **(first if hc == 0 else {}))
            b.emit("dve", (lambda o, i: lambda v: v.tensor_sub(o, i, mrb))(
                xn[hc], t1), reads=[B_t1, B_mrb], writes=[B_xn[hc]])

    def emit_router(xn, B_xn, ps_l, B_pl, ps_t, B_pt, ltT, B_ltT, mwT, B_mwT,
                    fence=()):
        for hc in range(HC):
            for tn in range(2):
                sl = slice(tn * 512, (tn + 1) * 512)
                b.emit("pe", mm_fn(ps_l[0:E, sl], gwT[hc][:, :], xn[hc][:, sl],
                                   hc == 0, hc == HC - 1),
                       reads=[B_gwT[hc], B_xn[hc]], writes=[B_pl])
        b.emit("dve", copy_fn(ltT, ps_l[0:E, :]), reads=[B_pl],
               writes=[B_ltT], deps=list(fence))
        for tt in range(8):
            b.emit("pe", tr_fn(ps_t[:, 0:E], ltT[:, tt * 128:(tt + 1) * 128],
                               ident[0:E, 0:E]),
                   reads=[B_ltT, B_ident], writes=[B_pt])
            b.emit("dve", copy_fn(l_sb[:, :], ps_t[:, 0:E]),
                   reads=[B_pt], writes=[B_tk])
            b.emit("dve", lambda v: v.max(out=topk8[:, :], in_=l_sb[:, :]),
                   reads=[B_tk], writes=[B_tk])
            b.emit("dve", lambda v: v.match_replace(
                out=zap[:, :], in_to_replace=topk8[:, :], in_values=l_sb[:, :],
                imm_value=-1e30), reads=[B_tk], writes=[B_tk])
            b.emit("dve", lambda v: v.tensor_sub(selm[:, :], l_sb[:, :],
                                                 zap[:, :]),
                   reads=[B_tk], writes=[B_tk])
            b.emit("dve", lambda v: v.tensor_scalar_min(selm[:, :], selm[:, :],
                                                        1.0),
                   reads=[B_tk], writes=[B_tk])
            b.emit("dve", lambda v: v.reduce_max(nmax[:, :], l_sb[:, :],
                                                 axis=AX.X, negate=True),
                   reads=[B_tk], writes=[B_tk])
            b.emit("act", lambda s: s.activation(expl[:, :], l_sb[:, :], AF.Exp,
                                                 bias=nmax[:, :]),
                   reads=[B_tk], writes=[B_tk])
            b.emit("dve", lambda v: v.tensor_mul(expl[:, :], expl[:, :],
                                                 selm[:, :]),
                   reads=[B_tk], writes=[B_tk])
            b.emit("dve", lambda v: v.reduce_sum(zsum[:, :], expl[:, :],
                                                 axis=AX.X),
                   reads=[B_tk], writes=[B_tk])
            b.emit("dve", lambda v: v.reciprocal(rz1[:, :], zsum[:, :]),
                   reads=[B_tk], writes=[B_tk])
            b.emit("dve", lambda v: v.tensor_scalar(
                out=mw_sb[:, :], in0=expl[:, :], scalar1=rz1[:, :],
                scalar2=None, op0=ALU.mult), reads=[B_tk], writes=[B_tk])
            b.emit("pe", tr_fn(ps_t[0:E, 0:128], mw_sb[:, :], ident[:, :]),
                   reads=[B_tk, B_ident], writes=[B_pt])
            b.emit("dve", (lambda c: lambda v: v.tensor_copy(
                mwT[:, c * 128:(c + 1) * 128], ps_t[0:E, 0:128]))(tt),
                reads=[B_pt], writes=[B_mwT])
        b.emit("pool", dma_fn(mw_dram[:, :], mwT), reads=[B_mwT],
               writes=[B_mwd], dma=True)

    def emit_ffn(xn, B_xn, gu_half, B_guh, dn_half, B_dnh, actw, B_actw,
                 mwb, B_mwb, g_s, B_gs, prod, B_prod, first_call, fence):
        gu_tiles = {}
        dn_tiles = {}
        GU_SUB = [(0, 0), (1, 0), (0, 1), (1, 1)]

        def gu_gen(g):
            if g >= 4 * E:
                return
            e, sub = divmod(g, 4)
            mat, ih = GU_SUB[sub]
            half = g % 3
            slab, bufs = gu_half[half], B_guh[half]
            for hc in range(HC):
                gu_tiles[(e, mat, ih, hc)] = (
                    slab[:, hc * 384:(hc + 1) * 384], bufs[hc])
            cdst = wguT_c[mat, e, ih]
            if first_call:
                wsrc = (wg_b, wu_b)[mat]
                bsrc = (B_wgb, B_wub)[mat]
                for hc in range(HC):
                    srca = wsrc[e, ih * 384:(ih + 1) * 384,
                                hc * 128:(hc + 1) * 128]
                    b.emit("sp", dma_fn(slab[:, hc * 384:(hc + 1) * 384],
                                        srca, tr=True),
                           reads=[bsrc[e]], writes=[bufs[hc]], dma=True)
                b.emit("pool", dma_fn(cdst, slab),
                       reads=list(bufs), writes=[B_wguT[mat][e]], dma=True)
            else:
                b.emit("sp", dma_fn(slab, cdst),
                       reads=[B_wguT[mat][e]], writes=list(bufs), dma=True)

        def dn_gen(g):
            if g >= 4 * E:
                return
            e, hcg2 = divmod(g, 4)
            half = g % 2
            slab, bufs = dn_half[half], B_dnh[half]
            for ic in range(IC):
                for hcl2 in range(2):
                    dn_tiles[(e, hcg2 * 2 + hcl2, ic)] = (
                        slab[:, ic * 512 + hcl2 * 256:
                             ic * 512 + (hcl2 + 1) * 256], bufs[ic])
            cdst = wdT_c[e, hcg2]
            if first_call:
                for ic in range(IC):
                    srca = wd_b[e, hcg2 * 512:(hcg2 + 1) * 512,
                                ic * 128:(ic + 1) * 128]
                    b.emit("sp", dma_fn(slab[:, ic * 512:(ic + 1) * 512],
                                        srca, tr=True),
                           reads=[B_wdb[e]], writes=[bufs[ic]], dma=True)
                b.emit("pool", dma_fn(cdst, slab),
                       reads=list(bufs), writes=[B_wdT[e]], dma=True)
            else:
                b.emit("sp", dma_fn(slab, cdst),
                       reads=[B_wdT[e]], writes=list(bufs), dma=True)

        def mw_stream(e):
            b.emit("pool", (lambda ee: lambda g: g.dma_start(
                out=mwb[ee % 2],
                in_=mw_dram[ee:ee + 1, :].partition_broadcast(128)))(e),
                reads=[B_mwd], writes=[B_mwb[e % 2]], dma=True,
                deps=(fence if e < 2 else ()))

        gu_gen(0)
        gu_gen(1)
        gu_gen(2)
        dn_gen(0)
        dn_gen(1)
        mw_stream(0)
        mw_stream(1)

        def gu_block(e):
            par = e % 2
            for ic in range(IC):
                ih, icl = ic // 3, ic % 3
                for mat, ps, B_ps in ((0, psA, B_A), (1, psB, B_B)):
                    for hc in range(HC):
                        ap, bb = gu_tiles[(e, mat, ih, hc)]
                        lhs = ap[:, icl * 128:(icl + 1) * 128]
                        for tn in range(2):
                            sl = slice(tn * 512, (tn + 1) * 512)
                            b.emit("pe", mm_fn(ps[:, sl], lhs, xn[hc][:, sl],
                                               hc == 0, hc == HC - 1),
                                   reads=[bb, B_xn[hc]], writes=[B_ps])
                b.emit("act", (lambda gg: lambda s: s.activation(
                    gg, psA[:, :], AF.Silu))(g_s),
                    reads=[B_A], writes=[B_gs],
                    deps=(fence if e == 0 and ic == 0 else ()))
                b.emit("dve", (lambda gg, pp: lambda v: v.scalar_tensor_tensor(
                    out=pp, in0=psB[:, :], scalar=1.0, in1=gg,
                    op0=ALU.mult, op1=ALU.mult))(g_s, prod),
                    reads=[B_B, B_gs], writes=[B_prod],
                    deps=(fence if e == 0 and ic == 0 else ()))
                b.emit("dve", (lambda o, pp, mb: lambda v: v.tensor_mul(
                    o, pp, mb))(actw[par][ic], prod, mwb[par]),
                    reads=[B_prod, B_mwb[par]], writes=[B_actw[par][ic]])
                if ic == 2:
                    gu_gen(4 * e + 3)
                    gu_gen(4 * e + 4)
                if ic == 5:
                    gu_gen(4 * e + 5)
                    gu_gen(4 * e + 6)

        def dn_block(e):
            par = e % 2
            for hcg in range(8):
                for hcl in range(2):
                    hc = hcg * 2 + hcl
                    ps, B_ps = (psC, B_C) if hc % 2 == 0 else (psD, B_D)
                    for ic in range(IC):
                        ap, bb = dn_tiles[(e, hcg, ic)]
                        lhs = ap[:, hcl * 128:(hcl + 1) * 128]
                        for tn in range(2):
                            sl = slice(tn * 512, (tn + 1) * 512)
                            b.emit("pe", mm_fn(ps[:, sl], lhs,
                                               actw[par][ic][:, sl],
                                               ic == 0, ic == IC - 1),
                                   reads=[bb, B_actw[par][ic]], writes=[B_ps])
                    b.emit("dve", (lambda hh, ps_: lambda v:
                                   v.scalar_tensor_tensor(
                                       out=X[hh], in0=ps_[:, :], scalar=0.5,
                                       in1=X[hh], op0=ALU.mult, op1=ALU.add))(
                        hc, ps), reads=[B_ps, B_X[hc]], writes=[B_X[hc]])
                if hcg % 2 == 1:
                    dn_gen(4 * e + hcg // 2 + 2)

        for e in range(E):
            if e + 1 < E:
                mw_stream(e + 1)
            if e + 2 < E and first_call:
                emit_casts(e + 2)
            if first_call and e == 24 and stage >= 2:
                emit_qkv_casts()
            gu_block(e)
            if e > 0:
                dn_block(e - 1)
        dn_block(E - 1)

    def ffn_phase(tag, first_call):
        AB.mark()
        AR.mark()
        xn = [AB.alloc(T) for _ in range(HC)]
        B_xn = [b.buf(f"xn{tag}_{hc}") for hc in range(HC)]
        xh = AB.alloc(T)
        B_xh = b.buf(f"xh{tag}")
        sq = AB.alloc(T)
        B_sq = b.buf(f"sq{tag}")
        gu_half = [AB.alloc(HC * 384) for _ in range(3)]
        B_guh = [[b.buf(f"gur{tag}_{h}_{i}") for i in range(HC)]
                 for h in range(3)]
        dn_half = [AB.alloc(IC * 512) for _ in range(2)]
        B_dnh = [[b.buf(f"dnr{tag}_{h}_{i}") for i in range(IC)]
                 for h in range(2)]
        actw = [[AB.alloc(T) for _ in range(IC)] for _ in range(2)]
        B_actw = [[b.buf(f"actw{tag}_{p}_{ic}") for ic in range(IC)]
                  for p in range(2)]
        AR.mark()
        t1 = AR.alloc(T)[:, :]
        B_t1 = b.buf(f"t1{tag}")
        Ra = AR.alloc(T)[0:1, :]
        Rb = AR.alloc(T)[0:1, :]
        Rc = AR.alloc(T)[0:1, :]
        B_rows = b.buf(f"rows{tag}")
        rbb = AR.alloc(T)
        B_rb = b.buf(f"rb{tag}")
        mrb = AR.alloc(T)
        B_mrb = b.buf(f"mrb{tag}")
        ltT = t1[0:E, :]
        B_ltT = b.buf(f"ltT{tag}")
        mwT = rbb[0:E, :]
        B_mwT = b.buf(f"mwT{tag}")

        emit_ln(xn, B_xn, xh, B_xh, sq, B_sq, t1, B_t1, Ra, Rb, Rc, B_rows,
                rbb, B_rb, mrb, B_mrb, psC, B_C, psD, B_D)
        emit_router(xn, B_xn, psC, B_C, psD, B_D, ltT, B_ltT, mwT, B_mwT,
                    fence=b.fence_deps())
        fence = b.fence_deps()
        AR.reset()
        AR.mark()
        mwb = [AR.alloc(T) for _ in range(2)]
        B_mwb = [b.buf(f"mwb{tag}_{i}") for i in range(2)]
        g_s = AR.alloc(T)
        B_gs = b.buf(f"g_s{tag}")
        prod = AR.alloc(T)
        B_prod = b.buf(f"prod{tag}")
        emit_ffn(xn, B_xn, gu_half, B_guh, dn_half, B_dnh, actw, B_actw,
                 mwb, B_mwb, g_s, B_gs, prod, B_prod, first_call, fence)
        AR.reset()
        AB.reset()
        AR.reset()

    ffn_phase("1", True)

    if stage >= 2:
        b.barrier()
        AB.mark()
        AR.mark()
        qT = [AB.alloc(T) for _ in range(HC)]
        B_qT = [b.buf(f"qT{i}") for i in range(HC)]

        AB.mark()
        AR.mark()
        xn2 = [AB.alloc(T) for _ in range(HC)]
        B_xn2 = [b.buf(f"xn2_{hc}") for hc in range(HC)]
        xh2 = AB.alloc(T)
        B_xh2 = b.buf("xh2")
        sq2 = AB.alloc(T)
        B_sq2 = b.buf("sq2")
        qkv_ring = Ring([(AB.alloc(512), b.buf(f"qkr{i}")) for i in range(24)])
        kv_stage = [AB.alloc(512) for _ in range(2)]
        B_kvs = [b.buf(f"kvs{i}") for i in range(2)]
        t12 = AR.alloc(T)[:, :]
        B_t12 = b.buf("t12")
        Ra2 = AR.alloc(T)[0:1, :]
        Rb2 = AR.alloc(T)[0:1, :]
        Rc2 = AR.alloc(T)[0:1, :]
        B_rows2 = b.buf("rows2")
        rbb2 = AR.alloc(T)
        B_rb2 = b.buf("rb2")
        mrb2 = AR.alloc(T)
        B_mrb2 = b.buf("mrb2")

        emit_ln(xn2, B_xn2, xh2, B_xh2, sq2, B_sq2, t12, B_t12,
                Ra2, Rb2, Rc2, B_rows2, rbb2, B_rb2, mrb2, B_mrb2,
                psA, B_A, psB, B_B)

        qkv_tiles = {}
        og_order = [4, 5, 6, 7, 8, 9, 10, 11, 0, 1, 2, 3]
        halves = [(og, h) for og in og_order for h in range(2)]
        half_i = [0]

        def qkv_half_gen():
            if half_i[0] >= len(halves):
                return
            og, h = halves[half_i[0]]
            half_i[0] += 1
            for hcl in range(8):
                hc = h * 8 + hcl
                ap, bb = qkv_ring.next()
                src = wqkv_b[og * 512:(og + 1) * 512, hc * 128:(hc + 1) * 128]
                b.emit(b.q("qkv"), dma_fn(ap, src, tr=True),
                       reads=[B_wqkvb[og]], writes=[bb], dma=True)
                qkv_tiles[(og, hc)] = (ap, bb)

        qkv_half_gen()
        qkv_half_gen()
        qkv_half_gen()

        for oi, og in enumerate(og_order):
            for ocl in range(4):
                oc = og * 4 + ocl
                ps, B_ps = (psA, B_A) if oc % 2 == 0 else (psB, B_B)
                for hc in range(HC):
                    ap, bb = qkv_tiles[(og, hc)]
                    lhs = ap[:, ocl * 128:(ocl + 1) * 128]
                    for tn in range(2):
                        sl = slice(tn * 512, (tn + 1) * 512)
                        b.emit("pe", mm_fn(ps[:, sl], lhs, xn2[hc][:, sl],
                                           hc == 0, hc == HC - 1),
                               reads=[bb, B_xn2[hc]], writes=[B_ps])
                if oc < 16:
                    b.emit("dve", (lambda o, ps_, c: lambda v: v.tensor_scalar(
                        out=o, in0=ps_[:, :], scalar1=bq_cols[:, c:c + 1],
                        scalar2=INV_SQRT_H, op0=ALU.add, op1=ALU.mult))(
                        qT[oc], ps, oc),
                        reads=[B_ps, B_bq], writes=[B_qT[oc]])
                else:
                    dstd = kv_in
                    dstb = B_kvin
                    r0 = (oc - 16) * 128 if oc < 32 else H + (oc - 32) * 128
                    for tn in range(2):
                        sl = slice(tn * 512, (tn + 1) * 512)
                        st, stb = kv_stage[tn], B_kvs[tn]
                        b.emit("dve", (lambda o, ps_, c, s_: lambda v:
                                       v.tensor_scalar(
                                           out=o, in0=ps_[:, s_],
                                           scalar1=bq_cols[:, c:c + 1],
                                           scalar2=None, op0=ALU.add))(
                            st, ps, oc, sl),
                            reads=[B_ps, B_bq], writes=[stb])
                        b.emit("sp", (lambda d, r, s_, st_: lambda s:
                                      s.dma_start(out=d[r:r + 128, s_],
                                                  in_=st_))(dstd, r0, sl, st),
                               reads=[stb], writes=[dstb], dma=True)
            qkv_half_gen()
            qkv_half_gen()
            if og == 11 and ATTN_UNTIL >= 2:
                b.emit("pool", lambda g: g.collective_compute(
                    "AllGather", ALU.bypass,
                    replica_groups=[[0, 1, 2, 3], [4, 5, 6, 7]],
                    ins=[kv_in.opt()], outs=[kv_out.opt()]),
                    reads=[B_kvin], writes=[B_kvout], cc=True)

        if ATTN_UNTIL >= 3:
            vstage = [AB.alloc(H) for _ in range(2)]
            B_vst = [b.buf(f"vst{i}") for i in range(2)]
            for kc in range(32):
                rk, tl = divmod(kc, 8)
                st, stb = vstage[kc % 2], B_vst[kc % 2]
                b.emit("sp", dma_fn(
                    st, kv_out[rk * 2 * H + H: (rk + 1) * 2 * H,
                               tl * 128:(tl + 1) * 128], tr=True),
                    reads=[B_kvout], writes=[stb], dma=True)
                b.emit("act", (lambda kc_, st_: lambda s: s.dma_start(
                    out=v_tm[kc_ * 128:(kc_ + 1) * 128, :], in_=st_))(kc, st),
                    reads=[stb], writes=[B_vtm], dma=True)

        AB.reset()
        AR.reset()
        b.barrier()

        AB.mark()
        AR.mark()
        ctxT_all = [AB.alloc(T) for _ in range(HC)]
        B_ctxT = [b.buf(f"ctxT{hc}") for hc in range(HC)]
        AB.mark()
        PT = AB.alloc(4096)
        B_PT = [b.buf(f"PT{i}") for i in range(4)]
        kt_ring = Ring([(AB.alloc(2 * T), b.buf(f"ktr{i}")) for i in range(3)])
        v_ring = Ring([(AB.alloc(2 * H), b.buf(f"vr{i}")) for i in range(2)])
        S_sb = AR.alloc(4096)
        B_S = b.buf("S")
        e_f32 = [AR.alloc(T)]
        B_e = [b.buf("ef0")]
        ctx_f32 = AR.alloc(H)
        B_ctx = b.buf("ctx")
        nmax2 = AR.alloc(1)
        B_nm2 = b.buf("nm2")
        zacc = AR.alloc(4)
        B_za = b.buf("za")
        rz2 = AR.alloc(1)
        B_rz2 = b.buf("rz2")

        kt_tiles = {}
        kt_gi = [0]

        def kt_gen():
            g = kt_gi[0]
            kt_gi[0] += 1
            if g >= 8 * 4 * 8:
                return
            qt, r = divmod(g, 32)
            kb, ocg = divmod(r, 8)
            ap, bb = kt_ring.next()
            rows = kv_out[kb * 2 * H + ocg * 256:
                          kb * 2 * H + (ocg + 1) * 256, :]
            b.emit("sp", dma_fn(ap.rearrange("p (g t) -> p g t", g=2),
                                rows.rearrange("(g p) t -> p g t", p=128)),
                   reads=[B_kvout], writes=[bb], dma=True)
            for ocl in range(2):
                kt_tiles[(qt, kb, ocg * 2 + ocl)] = (
                    ap[:, ocl * T:(ocl + 1) * T], bb)

        v_tiles = {}
        v_gi = [0]

        def v_gen():
            g = v_gi[0]
            v_gi[0] += 1
            if g >= 8 * 16:
                return
            qt, kcp = divmod(g, 16)
            ap, bb = v_ring.next()
            rows = v_tm[kcp * 256:(kcp + 1) * 256, :]
            b.emit("sp", dma_fn(ap.rearrange("p (g h) -> p g h", g=2),
                                rows.rearrange("(g p) h -> p g h", p=128)),
                   reads=[B_vtm], writes=[bb], dma=True)
            for kcl in range(2):
                v_tiles[(qt, kcp * 2 + kcl)] = (
                    ap[:, kcl * H:(kcl + 1) * H], bb)

        if ATTN_UNTIL >= 3:
            kt_gen()
            kt_gen()
            kt_gen()
            v_gen()
            v_gen()

        for qt in range(8 if ATTN_UNTIL >= 3 else 0):
            for kb in range(4):
                ps, B_ps = (psA, B_A) if kb % 2 == 0 else (psB, B_B)
                for ocg in range(8):
                    for ocl in range(2):
                        oc = ocg * 2 + ocl
                        ap, bb = kt_tiles[(qt, kb, oc)]
                        lhs = qT[oc][:, qt * 128:(qt + 1) * 128]
                        for kn in range(2):
                            sl = slice(kn * 512, (kn + 1) * 512)
                            b.emit("pe", mm_fn(ps[:, sl], lhs, ap[:, sl],
                                               oc == 0, oc == HC - 1),
                                   reads=[bb, B_qT[oc]], writes=[B_ps])
                    kt_gen()
                b.emit("dve", (lambda kb_, ps_: lambda v: v.tensor_copy(
                    S_sb[:, kb_ * T:(kb_ + 1) * T], ps_[:, :]))(kb, ps),
                    reads=[B_ps], writes=[B_S])
            b.emit("dve", lambda v: v.reduce_max(nmax2, S_sb, axis=AX.X,
                                                 negate=True),
                   reads=[B_S], writes=[B_nm2])
            for kb in range(4):
                et, eb = e_f32[0], B_e[0]
                b.emit("act", (lambda o, kb_: lambda s: s.activation(
                    o, S_sb[:, kb_ * T:(kb_ + 1) * T], AF.Exp, bias=nmax2))(
                    et, kb), reads=[B_S, B_nm2], writes=[eb])
                b.emit("dve", (lambda i, kb_: lambda v: v.reduce_sum(
                    zacc[:, kb_:kb_ + 1], i, axis=AX.X))(et, kb),
                    reads=[eb], writes=[B_za])
                for blk in range(8):
                    kc = kb * 8 + blk
                    pd, B_pd = ((psC[:, 0:512], B_Ca) if kc % 2 == 0
                                else (psC[:, 512:1024], B_Cb))
                    b.emit("pe", tr_fn(pd[:, 0:128],
                                       et[:, blk * 128:(blk + 1) * 128],
                                       ident[:, :]),
                           reads=[eb, B_ident], writes=[B_pd])
                    b.emit("dve", (lambda kc_, pd_: lambda v: v.tensor_copy(
                        PT[:, kc_ * 128:(kc_ + 1) * 128], pd_[:, 0:128]))(
                        kc, pd), reads=[B_pd], writes=[B_PT[kb]])
            b.emit("dve", lambda v: v.reduce_sum(rz2, zacc, axis=AX.X),
                   reads=[B_za], writes=[B_rz2])
            b.emit("dve", lambda v: v.reciprocal(rz2, rz2),
                   reads=[B_rz2], writes=[B_rz2])
            for kc in range(32):
                ap, bb = v_tiles[(qt, kc)]
                lhs = PT[:, kc * 128:(kc + 1) * 128]
                for hn in range(2):
                    ps, B_ps = (psA, B_A) if hn == 0 else (psB, B_B)
                    for tn in range(2):
                        b.emit("pe", mm_fn(
                            ps[:, tn * 512:(tn + 1) * 512], lhs,
                            ap[:, hn * 1024 + tn * 512:
                               hn * 1024 + (tn + 1) * 512],
                            kc == 0, kc == 31),
                            reads=[bb, B_PT[kc // 8]], writes=[B_ps])
                if kc % 2 == 1:
                    v_gen()
            for hn in range(2):
                ps, B_ps = (psA, B_A) if hn == 0 else (psB, B_B)
                b.emit("dve", (lambda hn_, ps_: lambda v: v.tensor_scalar(
                    out=ctx_f32[:, hn_ * 1024:(hn_ + 1) * 1024],
                    in0=ps_[:, :], scalar1=rz2, scalar2=None, op0=ALU.mult))(
                    hn, ps), reads=[B_ps, B_rz2], writes=[B_ctx])
            for hc in range(HC):
                pd, B_pd = ((psD[:, 0:512], B_Da) if hc % 2 == 0
                            else (psD[:, 512:1024], B_Db))
                b.emit("pe", tr_fn(pd[:, 0:128],
                                   ctx_f32[:, hc * 128:(hc + 1) * 128],
                                   ident[:, :]),
                       reads=[B_ctx, B_ident], writes=[B_pd])
                b.emit("dve", (lambda hc_, qt_, pd_: lambda v: v.tensor_copy(
                    ctxT_all[hc_][:, qt_ * 128:(qt_ + 1) * 128], pd_[:, 0:128]))(
                    hc, qt, pd), reads=[B_pd], writes=[B_ctxT[hc]])

        AB.reset()
        wo_ring = Ring([(AB.alloc(512), b.buf(f"wor{i}")) for i in range(24)])
        wo_fence = b.fence_deps()
        wo_tiles = {}
        wo_hi = [0]

        def wo_half_gen():
            if wo_hi[0] >= 8:
                return
            og, h = divmod(wo_hi[0], 2)
            first = wo_hi[0] <= 1
            wo_hi[0] += 1
            for hcl in range(8):
                hc = h * 8 + hcl
                ap, bb = wo_ring.next()
                src = wout_b[og * 512:(og + 1) * 512, hc * 128:(hc + 1) * 128]
                b.emit(b.q("wo"), dma_fn(ap, src, tr=True),
                       reads=[B_woutb[og]], writes=[bb], dma=True,
                       deps=(wo_fence if first else ()))
                wo_tiles[(og, hc)] = (ap, bb)

        if ATTN_UNTIL >= 4:
            wo_half_gen()
            wo_half_gen()
            wo_half_gen()
        for og in range(4 if ATTN_UNTIL >= 4 else 0):
            for ocl in range(4):
                oc = og * 4 + ocl
                ps, B_ps = (psA, B_A) if oc % 2 == 0 else (psB, B_B)
                for hc in range(HC):
                    ap, bb = wo_tiles[(og, hc)]
                    lhs = ap[:, ocl * 128:(ocl + 1) * 128]
                    for tn in range(2):
                        sl = slice(tn * 512, (tn + 1) * 512)
                        b.emit("pe", mm_fn(ps[:, sl], lhs,
                                           ctxT_all[hc][:, sl],
                                           hc == 0, hc == HC - 1),
                               reads=[bb, B_ctxT[hc]], writes=[B_ps])
                b.emit("dve", (lambda oc_, ps_: lambda v:
                               v.scalar_tensor_tensor(
                                   out=X[oc_], in0=ps_[:, :],
                                   scalar=bo_cols[:, oc_:oc_ + 1],
                                   in1=X[oc_], op0=ALU.add, op1=ALU.add))(
                    oc, ps), reads=[B_ps, B_X[oc], B_bo], writes=[B_X[oc]])
            wo_half_gen()
            wo_half_gen()
        AB.reset()
        AR.reset()
        AB.reset()
        AR.reset()

    if stage >= 3:
        b.barrier()
        ffn_phase("3", False)

    out_ops = []
    B_outs = [b.buf(f"outd{hc}") for hc in range(HC)]
    for hc in range(HC):
        op = b.emit("sp", dma_fn(out[hc * 128:(hc + 1) * 128, :], X[hc]),
                    reads=[B_X[hc]], writes=[B_outs[hc]], dma=True)
        out_ops.append(op)
    b.emit("pool", lambda g: g.engine_nop(), deps=out_ops)

    b.run_block()
    return nc


class Ring:
    def __init__(self, items):
        self.items = items
        self.i = 0

    def next(self):
        r = self.items[self.i % len(self.items)]
        self.i += 1
        return r


_NC_CACHE = {}


def kernel(**inputs):
    x = np.asarray(inputs["L0_attn_hidden_states"], dtype=np.float32)
    xf = np.ascontiguousarray(x.reshape(8192, 2048))
    stage = DEBUG_STAGE
    if stage not in _NC_CACHE:
        _NC_CACHE[stage] = build_nc(stage)
    nc = _NC_CACHE[stage]
    base = {
        "wqkv": np.asarray(inputs["L0_attn_w_qkv"], dtype=np.float32),
        "bqkv": np.asarray(inputs["L0_attn_b_qkv"], dtype=np.float32),
        "wout": np.asarray(inputs["L0_attn_w_out"], dtype=np.float32),
        "bout": np.asarray(inputs["L0_attn_b_out"], dtype=np.float32),
        "gw": np.asarray(inputs["L0_ffn_gate_w"], dtype=np.float32),
        "wg": np.asarray(inputs["L0_ffn_w_gate_proj"], dtype=np.float32),
        "wu": np.asarray(inputs["L0_ffn_w_up_proj"], dtype=np.float32),
        "wd": np.asarray(inputs["L0_ffn_w_down_proj"], dtype=np.float32),
    }
    in_maps = []
    for c in range(8):
        m = dict(base)
        m["x"] = np.ascontiguousarray(xf[c * 1024:(c + 1) * 1024, :].T)
        in_maps.append(m)
    res = run_bass_kernel_spmd(nc, in_maps, core_ids=list(range(8)))
    parts = [np.ascontiguousarray(res.results[c]["out"].T) for c in range(8)]
    return np.concatenate(parts, axis=0).reshape(2, 4096, 2048)
